# version 02 (final)
# baseline (speedup 1.0000x reference)
import contextlib
import numpy as np
import ml_dtypes

import concourse.bass as bass
import concourse.mybir as mybir
from concourse.bass_utils import run_bass_kernel_spmd
from concourse.aot_env import allow_ici_all_to_all

F32 = mybir.dt.float32
BF16 = mybir.dt.bfloat16
AF = mybir.ActivationFunctionType
ALU = mybir.AluOpType
BFNP = ml_dtypes.bfloat16

D = 2048
HD = 128
NCORE = 8
NH_CORE = 2
DFF = 8192
QT = 512
KT = 128
EPS_LN = 1e-5
EPS_ATT = 1e-6


class Sem:

    def __init__(self, es, nc, name):
        self.h = es.enter_context(nc.semaphore(name))
        self.v = 0

    def inc(self, inst, n=1):
        inst.then_inc(self.h, n)
        self.v += n
        return self.v

    def wait(self, eng, v=None):
        eng.wait_ge(self.h, self.v if v is None else v)


def pap(t, offset, dims):
    pstep = t.ap().ap[0][0]
    return bass.AP(t, offset, [[pstep, 128], *dims])


def build_nc(SEQ):
    ROWS = SEQ // NCORE
    NTT = SEQ // 128
    NQT = SEQ // QT
    NTC = ROWS // 128
    NHC = DFF // 128
    NOC = D // 128

    nc = bass.Bass("TRN2", target_bir_lowering=False)

    x_d = nc.dram_tensor("x", [SEQ, D], F32, kind="ExternalInput")
    xr_d = nc.dram_tensor("xr", [ROWS, D], F32, kind="ExternalInput")
    wq_d = nc.dram_tensor("wq", [D, 256], F32, kind="ExternalInput")
    wk_d = nc.dram_tensor("wk", [D, 256], F32, kind="ExternalInput")
    wv_d = nc.dram_tensor("wv", [D, 256], F32, kind="ExternalInput")
    wo_d = nc.dram_tensor("wo", [D, D], F32, kind="ExternalInput")
    w0_d = nc.dram_tensor("w0", [NHC, D, 128], F32, kind="ExternalInput")
    w1_d = nc.dram_tensor("w1", [NOC, DFF, 128], F32, kind="ExternalInput")
    b0_d = nc.dram_tensor("b0", [128, NHC], F32, kind="ExternalInput")
    b1_d = nc.dram_tensor("b1", [128, NOC], F32, kind="ExternalInput")
    cos_d = nc.dram_tensor("cosT", [NTT, 128, 64], F32, kind="ExternalInput")
    sin_d = nc.dram_tensor("sinT", [NTT, 128, 64], F32, kind="ExternalInput")
    id_d = nc.dram_tensor("identb", [128, 128], BF16, kind="ExternalInput")
    mask_d = nc.dram_tensor("maskb", [128, 128], F32, kind="ExternalInput")
    ones_d = nc.dram_tensor("onesb", [128, 128], BF16, kind="ExternalInput")

    out_d = nc.dram_tensor("out", [ROWS, D], F32, kind="ExternalOutput")

    a2a_in = nc.dram_tensor("a2a_in", [NCORE, NH_CORE, 128, ROWS], BF16)
    a2a_out = nc.dram_tensor("a2a_out", [NCORE, NH_CORE, 128, ROWS], BF16)
    out1_d = nc.dram_tensor("out1_spill", [ROWS, D], BF16)

    es = contextlib.ExitStack()
    with es:
        def sb(name, shape, dt=F32):
            return es.enter_context(nc.sbuf_tensor(name, shape, dt))

        def ps(name, shape, dt=F32):
            return es.enter_context(nc.psum_tensor(name, shape, dt))

        sv = Sem(es, nc, "sv")
        ss = Sem(es, nc, "ss")
        st = Sem(es, nc, "st")
        sg = Sem(es, nc, "sg")
        qcst = Sem(es, nc, "qcst")
        gw = Sem(es, nc, "gw")
        gx = [Sem(es, nc, f"gx{i}") for i in range(2)]
        qc = Sem(es, nc, "qc")
        qa = [Sem(es, nc, f"qa{i}") for i in range(2)]
        gwo = Sem(es, nc, "gwo")
        qg = Sem(es, nc, "qg")
        qx = [Sem(es, nc, f"qx{i}") for i in range(2)]
        qs = [Sem(es, nc, f"qs{i}") for i in range(2)]
        g5 = [Sem(es, nc, f"g5{i}") for i in range(2)]
        g6 = [Sem(es, nc, f"g6{i}") for i in range(2)]
        q7 = [Sem(es, nc, f"q7{i}") for i in range(2)]
        q8 = [Sem(es, nc, f"q8{i}") for i in range(2)]

        id_sb = sb("id_sb", [128, 128], BF16)
        ones_sb = sb("ones_sb", [128, 128], BF16)
        mask_sb = sb("mask_sb", [128, 128], F32)
        b0_sb = sb("b0_sb", [128, NHC], F32)
        b1_sb = sb("b1_sb", [128, NOC], F32)
        cst = sb("cst", [128, 2], F32)
        nc.const_aps.register(F32, EPS_LN, cst[:, 0:1])
        nc.const_aps.register(F32, -1.0, cst[:, 1:2])

        esA = contextlib.ExitStack()

        def sbA(name, shape, dt=F32):
            return esA.enter_context(nc.sbuf_tensor(name, shape, dt))

        qf = sbA("qf", [128, NH_CORE * SEQ], BF16)
        kf = sbA("kf", [128, NH_CORE * SEQ], BF16)
        vt = sbA("vt", [128, NH_CORE * SEQ], BF16)

        with nc.Block() as blk:
            @blk.gpsimd
            def _(g):
                g.memset(cst[:, 0:1], EPS_LN)
                g.memset(cst[:, 1:2], -1.0)

            @blk.sync
            def _(q):
                qcst.inc(q.dma_start(out=id_sb[:, :], in_=id_d[:, :]), 16)
                qcst.inc(q.dma_start(out=ones_sb[:, :], in_=ones_d[:, :]), 16)
                qcst.inc(q.dma_start(out=mask_sb[:, :], in_=mask_d[:, :]), 16)
                qcst.inc(q.dma_start(out=b0_sb[:, :], in_=b0_d[:, :]), 16)
                qcst.inc(q.dma_start(out=b1_sb[:, :], in_=b1_d[:, :]), 16)
                qcst.wait(q)

        with contextlib.ExitStack() as es1:
            def sb1(name, shape, dt=F32):
                return es1.enter_context(nc.sbuf_tensor(name, shape, dt))

            def ps1(name, shape, dt=F32):
                return es1.enter_context(nc.psum_tensor(name, shape, dt))

            cos_sb = sb1("cos_sb", [128, NTT * 64], F32)
            sin_sb = sb1("sin_sb", [128, NTT * 64], F32)
            wq_sb = sb1("wq_sb", [128, 16 * 256], BF16)
            wk_sb = sb1("wk_sb", [128, 16 * 256], BF16)
            wv_sb = sb1("wv_sb", [128, 16 * 256], BF16)
            xt = [sb1(f"xt{i}", [128, D], BF16) for i in range(2)]
            xnt = [sb1(f"xnt{i}", [128, D], BF16) for i in range(2)]
            xnf = [sb1(f"xnf{i}", [128, D], BF16) for i in range(2)]
            junk = [sb1(f"junk{i}", [128, D], BF16) for i in range(2)]
            qrot = [sb1(f"qrot{i}", [128, 512], BF16) for i in range(2)]
            t14 = [sb1(f"t14_{i}", [128, 256], F32) for i in range(2)]
            sx = [sb1(f"sx{i}", [128, 1], F32) for i in range(2)]
            sxx = [sb1(f"sxx{i}", [128, 1], F32) for i in range(2)]
            negm = [sb1(f"negm{i}", [128, 1], F32) for i in range(2)]
            msq = [sb1(f"msq{i}", [128, 1], F32) for i in range(2)]
            var = [sb1(f"var{i}", [128, 1], F32) for i in range(2)]
            sd = [sb1(f"sd{i}", [128, 1], F32) for i in range(2)]
            rr = [sb1(f"rr{i}", [128, 1], F32) for i in range(2)]

            tps = [ps1(f"tps{i}", [128, D], BF16) for i in range(2)]
            qkv_ps = ps1("qkv_ps", [128, 1536], F32)
            tqr = ps1("tqr", [128, 512], BF16)

            ms = {k: {} for k in
                  ("gx", "stat", "var", "sd", "rr", "norm", "qkv", "rope",
                   "trxn", "evxn", "trqr", "evqr")}

            with nc.Block() as blk:
                @blk.gpsimd
                def _(g):
                    gw.inc(g.dma_start(
                        out=wq_sb[:, :].rearrange("p (c d) -> p c d", c=16),
                        in_=wq_d.ap().rearrange("(c p) d -> p c d", p=128)), 16)
                    gw.inc(g.dma_start(
                        out=wk_sb[:, :].rearrange("p (c d) -> p c d", c=16),
                        in_=wk_d.ap().rearrange("(c p) d -> p c d", p=128)), 16)
                    gw.inc(g.dma_start(
                        out=wv_sb[:, :].rearrange("p (c d) -> p c d", c=16),
                        in_=wv_d.ap().rearrange("(c p) d -> p c d", p=128)), 16)
                    for j in range(min(2, NTT)):
                        ms["gx"][j] = gx[j % 2].inc(g.dma_start(
                            out=xt[j % 2][:, :], in_=x_d[j * 128:(j + 1) * 128, :]), 16)

                @blk.sync
                def _(q):
                    qc.inc(q.dma_start(
                        out=cos_sb[:, :].rearrange("p (j t) -> p j t", j=NTT),
                        in_=cos_d.ap().rearrange("j p t -> p j t")), 16)
                    qc.inc(q.dma_start(
                        out=sin_sb[:, :].rearrange("p (j t) -> p j t", j=NTT),
                        in_=sin_d.ap().rearrange("j p t -> p j t")), 16)

                wq_ready = gw.v
                cs_ready = qc.v

                for j in range(NTT + 2):
                    jm1, jm2 = j - 1, j - 2
                    p = j % 2
                    if j < NTT:
                        @blk.scalar
                        def _(s, j=j, p=p):
                            s.wait_ge(gx[j % 2].h, ms["gx"][j])
                            if j >= 2:
                                s.wait_ge(sv.h, ms["var"][j - 2])
                            ss.inc(s.activation(junk[p][:, :], xt[p][:, :], AF.Square,
                                                accum_out=sxx[p][:, :]))
                            ss.wait(s)
                            ms["stat"][j] = ss.inc(
                                s.activation(junk[p][:, :], xt[p][:, :], AF.Identity,
                                             accum_out=sx[p][:, :]))

                    if j < NTT:
                        @blk.vector
                        def _(v, j=j, p=p):
                            v.wait_ge(ss.h, ms["stat"][j])
                            sv.inc(v.tensor_scalar(negm[p][:, :], sx[p][:, :],
                                                   -1.0 / D, None, ALU.mult))
                            sv.wait(v)
                            sv.inc(v.tensor_tensor(msq[p][:, :], negm[p][:, :],
                                                   negm[p][:, :], ALU.mult))
                            sv.wait(v)
                            ms["var"][j] = sv.inc(v.scalar_tensor_tensor(
                                var[p][:, :], sxx[p][:, :], 1.0 / D, msq[p][:, :],
                                ALU.mult, ALU.subtract))

                        @blk.scalar
                        def _(s, j=j, p=p):
                            s.wait_ge(sv.h, ms["var"][j])
                            ms["sd"][j] = ss.inc(s.activation(
                                sd[p][:, :], var[p][:, :], AF.Sqrt, bias=EPS_LN))

                        @blk.vector
                        def _(v, j=j, p=p):
                            v.wait_ge(ss.h, ms["sd"][j])
                            ms["rr"][j] = sv.inc(v.reciprocal(rr[p][:, :], sd[p][:, :]))
                            v.wait_ge(sv.h, ms["rr"][j])
                            if j >= 2:
                                v.wait_ge(st.h, ms["trxn"][j - 2])
                            ms["norm"][j] = sv.inc(v.tensor_scalar(
                                xnt[p][:, :], xt[p][:, :], negm[p][:, :], rr[p][:, :],
                                ALU.add, ALU.mult))

                    @blk.tensor
                    def _(t, j=j, jm1=jm1, jm2=jm2):
                        if 0 <= jm1 <= NTT - 1:
                            t.wait_ge(ss.h, ms["evxn"][jm1])
                            t.wait_ge(gw.h, wq_ready)
                            if jm1 >= 1:
                                t.wait_ge(sv.h, ms["rope"][jm1 - 1])
                            pp = jm1 % 2
                            for c in range(16):
                                lhs = xnf[pp][:, c * 128:(c + 1) * 128]
                                t.matmul(qkv_ps[:, 0:256], lhs,
                                         wq_sb[:, c * 256:(c + 1) * 256],
                                         start=(c == 0), stop=(c == 15))
                                t.matmul(qkv_ps[:, 512:768], lhs,
                                         wk_sb[:, c * 256:(c + 1) * 256],
                                         start=(c == 0), stop=(c == 15))
                                inst = t.matmul(qkv_ps[:, 1024:1280], lhs,
                                                wv_sb[:, c * 256:(c + 1) * 256],
                                                start=(c == 0), stop=(c == 15))
                            ms["qkv"][jm1] = st.inc(inst)
                        if 0 <= jm2 <= NTT - 1:
                            t.wait_ge(sv.h, ms["rope"][jm2])
                            if jm2 >= 1:
                                t.wait_ge(ss.h, ms["evqr"][jm2 - 1])
                            pp = jm2 % 2
                            for b in range(4):
                                inst = t.transpose(tqr[:, b * 128:(b + 1) * 128],
                                                   qrot[pp][:, b * 128:(b + 1) * 128],
                                                   id_sb[:, :])
                            ms["trqr"][jm2] = st.inc(inst)
                        if j <= NTT - 1:
                            t.wait_ge(sv.h, ms["norm"][j])
                            if j >= 2:
                                t.wait_ge(ss.h, ms["evxn"][j - 2])
                            pp = j % 2
                            for c in range(16):
                                inst = t.transpose(tps[pp][:, c * 128:(c + 1) * 128],
                                                   xnt[pp][:, c * 128:(c + 1) * 128],
                                                   id_sb[:, :])
                            ms["trxn"][j] = st.inc(inst)

                    if 0 <= jm1 <= NTT - 1:
                        @blk.vector
                        def _(v, j=jm1, p=jm1 % 2):
                            v.wait_ge(st.h, ms["qkv"][j])
                            v.wait_ge(qc.h, cs_ready)
                            if j >= 2:
                                v.wait_ge(st.h, ms["trqr"][j - 2])
                            q1 = pap(qkv_ps, 0, [[512, 2], [128, 2], [1, 64]])
                            q2 = pap(qkv_ps, 64, [[512, 2], [128, 2], [1, 64]])
                            cosb = pap(cos_sb, j * 64, [[0, 2], [0, 2], [1, 64]])
                            sinb = pap(sin_sb, j * 64, [[0, 2], [0, 2], [1, 64]])
                            t1 = pap(t14[p], 0, [[128, 2], [64, 2], [1, 64]])
                            t2 = pap(t14[(j + 1) % 2], 0, [[128, 2], [64, 2], [1, 64]])
                            r1 = pap(qrot[p], 0, [[256, 2], [128, 2], [1, 64]])
                            r2 = pap(qrot[p], 64, [[256, 2], [128, 2], [1, 64]])
                            sv.inc(v.tensor_tensor(t1, q1, cosb, ALU.mult))
                            sv.inc(v.tensor_tensor(t2, q2, sinb, ALU.mult))
                            sv.wait(v)
                            sv.inc(v.tensor_tensor(r1, t1, t2, ALU.subtract))
                            sv.wait(v)
                            sv.inc(v.tensor_tensor(t1, q1, sinb, ALU.mult))
                            sv.inc(v.tensor_tensor(t2, q2, cosb, ALU.mult))
                            sv.wait(v)
                            sv.inc(v.tensor_tensor(r2, t1, t2, ALU.add))
                            vsrc = pap(qkv_ps, 1024, [[128, 2], [1, 128]])
                            vdst = pap(vt, j * 128, [[SEQ, 2], [1, 128]])
                            ms["rope"][j] = sv.inc(v.tensor_copy(vdst, vsrc))

                    if j <= NTT - 1:
                        @blk.scalar
                        def _(s, j=j, p=p):
                            s.wait_ge(st.h, ms["trxn"][j])
                            ms["evxn"][j] = ss.inc(
                                s.copy(xnf[p][:, :], tps[p][:, :]))
                    if 0 <= jm2 <= NTT - 1:
                        @blk.scalar
                        def _(s, j=jm2):
                            s.wait_ge(st.h, ms["trqr"][j])
                            qsrc = pap(tqr, 0, [[128, 2], [1, 128]])
                            qdst = pap(qf, j * 128, [[SEQ, 2], [1, 128]])
                            ksrc = pap(tqr, 256, [[128, 2], [1, 128]])
                            kdst = pap(kf, j * 128, [[SEQ, 2], [1, 128]])
                            ss.inc(s.copy(qdst, qsrc))
                            ms["evqr"][j] = ss.inc(s.copy(kdst, ksrc))

                    if j + 2 <= NTT - 1:
                        @blk.gpsimd
                        def _(g, j=j):
                            g.wait_ge(ss.h, ms["stat"][j])
                            g.wait_ge(sv.h, ms["norm"][j])
                            ms["gx"][j + 2] = gx[j % 2].inc(g.dma_start(
                                out=xt[j % 2][:, :],
                                in_=x_d[(j + 2) * 128:(j + 3) * 128, :]), 16)

        with contextlib.ExitStack() as es2:
            def sb2(name, shape, dt=F32):
                return es2.enter_context(nc.sbuf_tensor(name, shape, dt))

            def ps2(name, shape, dt=F32):
                return es2.enter_context(nc.psum_tensor(name, shape, dt))

            s2b = [sb2(f"s2b{i}", [128, QT], F32) for i in range(2)]
            s4b = [sb2(f"s4b{i}", [128, QT], BF16) for i in range(2)]
            dmax = [sb2(f"dmax{i}", [128, QT], F32) for i in range(2)]
            lnd = [sb2(f"lnd{i}", [128, QT], F32) for i in range(2)]
            rec = [sb2(f"rec{i}", [128, QT], F32) for i in range(2)]
            att = [sb2(f"att{i}", [128, QT], BF16) for i in range(2)]

            s_ps = [ps2(f"s_ps{i}", [128, QT], F32) for i in range(2)]
            av_ps = [ps2(f"av_ps{i}", [128, QT], F32) for i in range(2)]
            dn_ps = [ps2(f"dn_ps{i}", [128, QT], F32) for i in range(2)]

            msa = {k: {} for k in ("qk", "s2", "s4", "av", "dn", "dm", "rec",
                                   "att", "dma")}

            with nc.Block() as blk:
                tile = 0
                for h in range(NH_CORE):
                    for qi in range(NQT):
                        qidx = h * NQT + qi
                        qp = qidx % 2
                        q0 = qi * QT
                        nchunks = (q0 + QT) // KT
                        for c in range(nchunks):
                            p = tile % 2
                            off = max(0, c * KT - q0)
                            npr = QT - off
                            diag = (c * KT >= q0)

                            @blk.tensor
                            def _(t, h=h, q0=q0, c=c, off=off, npr=npr, p=p,
                                  tile=tile, qidx=qidx, qp=qp, first=(c == 0),
                                  last=(c == nchunks - 1)):
                                if tile >= 2:
                                    t.wait_ge(ss.h, msa["s2"][tile - 2])
                                msa["qk"][tile] = st.inc(t.matmul(
                                    s_ps[p][:, 0:npr],
                                    kf[:, h * SEQ + c * KT: h * SEQ + (c + 1) * KT],
                                    qf[:, h * SEQ + q0 + off: h * SEQ + q0 + QT],
                                    start=True, stop=True))

                            @blk.scalar
                            def _(s, tile=tile, p=p, npr=npr):
                                s.wait_ge(st.h, msa["qk"][tile])
                                if tile >= 2:
                                    s.wait_ge(sv.h, msa["s4"][tile - 2])
                                msa["s2"][tile] = ss.inc(s.activation(
                                    s2b[p][:, 0:npr], s_ps[p][:, 0:npr], AF.Square))

                            @blk.vector
                            def _(v, tile=tile, p=p, npr=npr, diag=diag):
                                v.wait_ge(ss.h, msa["s2"][tile])
                                if tile >= 2:
                                    v.wait_ge(st.h, msa["dn"][tile - 2])
                                sv.inc(v.tensor_tensor(
                                    s4b[p][:, 0:npr], s2b[p][:, 0:npr],
                                    s2b[p][:, 0:npr], ALU.mult))
                                if diag:
                                    sv.wait(v)
                                    sv.inc(v.tensor_tensor(
                                        s4b[p][:, 0:128], s4b[p][:, 0:128],
                                        mask_sb[:, :], ALU.mult))
                                msa["s4"][tile] = sv.v

                            @blk.tensor
                            def _(t, h=h, c=c, off=off, npr=npr, p=p, tile=tile,
                                  qidx=qidx, qp=qp, first=(c == 0),
                                  last=(c == nchunks - 1)):
                                t.wait_ge(sv.h, msa["s4"][tile])
                                if first and qidx >= 2:
                                    t.wait_ge(sv.h, msa["att"][qidx - 2])
                                t.matmul(av_ps[qp][:, off:off + npr],
                                         vt[:, h * SEQ + c * KT: h * SEQ + (c + 1) * KT],
                                         s4b[p][:, 0:npr],
                                         start=first, stop=last)
                                msa["dn"][tile] = st.inc(t.matmul(
                                    dn_ps[qp][:, off:off + npr],
                                    ones_sb[:, :], s4b[p][:, 0:npr],
                                    start=first, stop=last))

                            tile += 1

                        @blk.vector
                        def _(v, qidx=qidx, qp=qp, last_tile=tile - 1):
                            v.wait_ge(st.h, msa["dn"][last_tile])
                            if qidx >= 2:
                                v.wait_ge(ss.h, msa["rec"][qidx - 2])
                            msa["dm"][qidx] = sv.inc(v.tensor_scalar(
                                dmax[qp][:, :], dn_ps[qp][:, :], EPS_ATT, None,
                                ALU.max))

                        @blk.scalar
                        def _(s, qidx=qidx, qp=qp):
                            s.wait_ge(sv.h, msa["dm"][qidx])
                            ss.inc(s.activation(lnd[qp][:, :], dmax[qp][:, :], AF.Ln))
                            ss.wait(s)
                            msa["rec"][qidx] = ss.inc(s.activation(
                                rec[qp][:, :], lnd[qp][:, :], AF.Exp, scale=-1.0))

                        @blk.vector
                        def _(v, qidx=qidx, qp=qp):
                            v.wait_ge(ss.h, msa["rec"][qidx])
                            if qidx >= 2:
                                v.wait_ge(qa[qidx % 2].h,
                                          msa["dma"][qidx - 2])
                            msa["att"][qidx] = sv.inc(v.tensor_tensor(
                                att[qp][:, :], av_ps[qp][:, :], rec[qp][:, :],
                                ALU.mult))

                        @blk.sync
                        def _(q, h=h, q0=q0, qidx=qidx, qp=qp):
                            q.wait_ge(sv.h, msa["att"][qidx])
                            nsh = max(1, QT // ROWS)
                            w = QT // nsh
                            for si in range(nsh):
                                t0 = q0 + si * w
                                inst = q.dma_start(
                                    out=a2a_in[t0 // ROWS, h, :,
                                               t0 % ROWS: t0 % ROWS + w],
                                    in_=att[qp][:, si * w:(si + 1) * w])
                                qa[qidx % 2].inc(inst, 16)
                            msa["dma"][qidx] = qa[qidx % 2].v

                @blk.gpsimd
                def _(g):
                    g.wait_ge(qa[0].h, qa[0].v)
                    g.wait_ge(qa[1].h, qa[1].v)
                    with allow_ici_all_to_all():
                        inst = g.collective_compute(
                            "AllToAll", ALU.bypass,
                            replica_groups=[list(range(NCORE))],
                            ins=[a2a_in.ap().opt()],
                            outs=[a2a_out.ap().opt()])
                    sg.inc(inst)

        esA.close()

        xnf2 = sb("xnf2", [128, 16 * ROWS], BF16)
        scope34 = contextlib.ExitStack()
        xn2t = scope34.enter_context(
            nc.sbuf_tensor("xn2t", [128, NTC * D], BF16))

        with contextlib.ExitStack() as es3:
            def sb3(name, shape, dt=F32):
                return es3.enter_context(nc.sbuf_tensor(name, shape, dt))

            def ps3(name, shape, dt=F32):
                return es3.enter_context(nc.psum_tensor(name, shape, dt))

            wo_sb = sb3("wo_sb", [128, 16 * D], BF16)
            a_sb = sb3("a_sb", [128, 16 * ROWS], BF16)
            xr_sb = [sb3(f"xr_sb{i}", [128, D], F32) for i in range(2)]
            n1 = [sb3(f"n1_{i}", [128, D], BF16) for i in range(2)]
            o1 = [sb3(f"o1_{i}", [128, D], BF16) for i in range(2)]
            junk3 = [sb3(f"junk3_{i}", [128, D], BF16) for i in range(2)]
            stt = [[sb3(f"stt{i}_{k}", [128, 1], F32) for k in range(8)]
                   for i in range(2)]

            p_ps = [ps3(f"p_ps{i}", [128, D], F32) for i in range(2)]

            msp = {k: {} for k in ("xr", "p", "stat", "var", "sd", "rr", "n1",
                                   "o1", "o1d", "stat2", "var2", "sd2", "rr2",
                                   "xn2")}

            with nc.Block() as blk:
                @blk.gpsimd
                def _(g):
                    g.wait_ge(sg.h, sg.v)
                    gwo.inc(g.dma_start(
                        out=wo_sb[:, :].rearrange("p (c d) -> p c d", c=16),
                        in_=wo_d.ap().rearrange("(c p) d -> p c d", p=128)), 16)
                wo_ready = gwo.v

                @blk.sync
                def _(q):
                    q.wait_ge(sg.h, sg.v)
                    for d in range(NCORE):
                        for hh in range(NH_CORE):
                            i = 2 * d + hh
                            qg.inc(q.dma_start(
                                out=a_sb[:, i * ROWS:(i + 1) * ROWS],
                                in_=a2a_out[d, hh, :, :]), 16)
                    for u in range(min(2, NTC)):
                        msp["xr"][u] = qx[u % 2].inc(q.dma_start(
                            out=xr_sb[u % 2][:, :],
                            in_=xr_d[u * 128:(u + 1) * 128, :]), 16)
                a_ready = qg.v

                for u in range(NTC):
                    up = u % 2

                    @blk.tensor
                    def _(t, u=u, up=up):
                        t.wait_ge(gwo.h, wo_ready)
                        t.wait_ge(qg.h, a_ready)
                        if u >= 2:
                            t.wait_ge(ss.h, msp["stat"][u - 2])
                            t.wait_ge(ss.h, msp["n1"][u - 2])
                        for i in range(16):
                            lhs = a_sb[:, i * ROWS + u * 128: i * ROWS + (u + 1) * 128]
                            for n in range(4):
                                inst = t.matmul(
                                    p_ps[up][:, n * 512:(n + 1) * 512], lhs,
                                    wo_sb[:, i * D + n * 512: i * D + (n + 1) * 512],
                                    start=(i == 0), stop=(i == 15))
                        msp["p"][u] = st.inc(inst)

                    @blk.scalar
                    def _(s, u=u, up=up):
                        s.wait_ge(st.h, msp["p"][u])
                        if u >= 2:
                            s.wait_ge(sv.h, msp["var2"][u - 2])
                        ss.inc(s.activation(junk3[up][:, :], p_ps[up][:, :],
                                            AF.Square, accum_out=stt[up][1][:, :]))
                        ss.wait(s)
                        msp["stat"][u] = ss.inc(s.activation(
                            junk3[up][:, :], p_ps[up][:, :], AF.Identity,
                            accum_out=stt[up][0][:, :]))

                    @blk.vector
                    def _(v, u=u, up=up):
                        v.wait_ge(ss.h, msp["stat"][u])
                        sv.inc(v.tensor_scalar(stt[up][2][:, :], stt[up][0][:, :],
                                               -1.0 / D, None, ALU.mult))
                        sv.wait(v)
                        sv.inc(v.tensor_tensor(stt[up][3][:, :], stt[up][2][:, :],
                                               stt[up][2][:, :], ALU.mult))
                        sv.wait(v)
                        msp["var"][u] = sv.inc(v.scalar_tensor_tensor(
                            stt[up][4][:, :], stt[up][1][:, :], 1.0 / D,
                            stt[up][3][:, :], ALU.mult, ALU.subtract))

                    @blk.scalar
                    def _(s, u=u, up=up):
                        s.wait_ge(sv.h, msp["var"][u])
                        msp["sd"][u] = ss.inc(s.activation(
                            stt[up][5][:, :], stt[up][4][:, :], AF.Sqrt, bias=EPS_LN))

                    @blk.vector
                    def _(v, u=u, up=up):
                        v.wait_ge(ss.h, msp["sd"][u])
                        sv.inc(v.reciprocal(stt[up][6][:, :], stt[up][5][:, :]))
                        sv.wait(v)
                        msp["rr"][u] = sv.inc(v.tensor_tensor(
                            stt[up][7][:, :], stt[up][2][:, :], stt[up][6][:, :],
                            ALU.mult))

                    @blk.scalar
                    def _(s, u=u, up=up):
                        s.wait_ge(sv.h, msp["rr"][u])
                        if u >= 2:
                            s.wait_ge(sv.h, msp["o1"][u - 2])
                        msp["n1"][u] = ss.inc(s.activation(
                            n1[up][:, :], p_ps[up][:, :], AF.Identity,
                            bias=stt[up][7][:, :], scale=stt[up][6][:, :]))

                    @blk.vector
                    def _(v, u=u, up=up):
                        v.wait_ge(ss.h, msp["n1"][u])
                        v.wait_ge(qx[u % 2].h, msp["xr"][u])
                        if u >= 2:
                            v.wait_ge(qs[u % 2].h, msp["o1d"][u - 2])
                        msp["o1"][u] = sv.inc(v.tensor_tensor(
                            o1[up][:, :], xr_sb[up][:, :], n1[up][:, :], ALU.add))

                    @blk.sync
                    def _(q, u=u, up=up):
                        q.wait_ge(sv.h, msp["o1"][u])
                        msp["o1d"][u] = qs[u % 2].inc(q.dma_start(
                            out=out1_d[u * 128:(u + 1) * 128, :],
                            in_=o1[up][:, :]), 16)
                        if u + 2 <= NTC - 1:
                            msp["xr"][u + 2] = qx[u % 2].inc(q.dma_start(
                                out=xr_sb[up][:, :],
                                in_=xr_d[(u + 2) * 128:(u + 3) * 128, :]), 16)

                    @blk.scalar
                    def _(s, u=u, up=up):
                        s.wait_ge(sv.h, msp["o1"][u])
                        ss.inc(s.activation(junk3[up][:, :], o1[up][:, :],
                                            AF.Square, accum_out=stt[up][1][:, :]))
                        ss.wait(s)
                        msp["stat2"][u] = ss.inc(s.activation(
                            junk3[up][:, :], o1[up][:, :], AF.Identity,
                            accum_out=stt[up][0][:, :]))

                    @blk.vector
                    def _(v, u=u, up=up):
                        v.wait_ge(ss.h, msp["stat2"][u])
                        sv.inc(v.tensor_scalar(stt[up][2][:, :], stt[up][0][:, :],
                                               -1.0 / D, None, ALU.mult))
                        sv.wait(v)
                        sv.inc(v.tensor_tensor(stt[up][3][:, :], stt[up][2][:, :],
                                               stt[up][2][:, :], ALU.mult))
                        sv.wait(v)
                        msp["var2"][u] = sv.inc(v.scalar_tensor_tensor(
                            stt[up][4][:, :], stt[up][1][:, :], 1.0 / D,
                            stt[up][3][:, :], ALU.mult, ALU.subtract))

                    @blk.scalar
                    def _(s, u=u, up=up):
                        s.wait_ge(sv.h, msp["var2"][u])
                        msp["sd2"][u] = ss.inc(s.activation(
                            stt[up][5][:, :], stt[up][4][:, :], AF.Sqrt, bias=EPS_LN))

                    @blk.vector
                    def _(v, u=u, up=up):
                        v.wait_ge(ss.h, msp["sd2"][u])
                        sv.inc(v.reciprocal(stt[up][6][:, :], stt[up][5][:, :]))
                        sv.wait(v)
                        msp["rr2"][u] = sv.v
                        msp["xn2"][u] = sv.inc(v.tensor_scalar(
                            xn2t[:, u * D:(u + 1) * D], o1[up][:, :],
                            stt[up][2][:, :], stt[up][6][:, :], ALU.add, ALU.mult))

                @blk.sync
                def _(q):
                    qs[0].wait(q)
                    qs[1].wait(q)

        with contextlib.ExitStack() as es4:
            tps4 = [es4.enter_context(nc.psum_tensor(f"tps4_{i}", [128, D], BF16))
                    for i in range(2)]
            mst = {"tr": {}, "ev": {}}
            with nc.Block() as blk:
                for u in range(NTC):
                    up = u % 2

                    @blk.tensor
                    def _(t, u=u, up=up):
                        if u >= 2:
                            t.wait_ge(sv.h, mst["ev"][u - 2])
                        for c in range(16):
                            inst = t.transpose(tps4[up][:, c * 128:(c + 1) * 128],
                                               xn2t[:, u * D + c * 128: u * D + (c + 1) * 128],
                                               id_sb[:, :])
                        mst["tr"][u] = st.inc(inst)

                    @blk.vector
                    def _(v, u=u, up=up):
                        v.wait_ge(st.h, mst["tr"][u])
                        dst = pap(xnf2, u * 128, [[ROWS, 16], [1, 128]])
                        src = pap(tps4[up], 0, [[128, 16], [1, 128]])
                        mst["ev"][u] = sv.inc(v.tensor_copy(dst, src))

        scope34.close()

        h_sb = sb("h_sb", [128, NHC * ROWS], BF16)
        n_tmm = (ROWS + 511) // 512

        with contextlib.ExitStack() as es5:
            w0_sb = [es5.enter_context(nc.sbuf_tensor(f"w0_sb{i}", [128, D], BF16))
                     for i in range(2)]
            h_ps = [es5.enter_context(nc.psum_tensor(f"h_ps{i}", [128, ROWS], F32))
                    for i in range(2)]
            msf = {"w0": {}, "mm": {}, "ev": {}}
            with nc.Block() as blk:
                @blk.gpsimd
                def _(g):
                    for hc in range(min(2, NHC)):
                        msf["w0"][hc] = g5[hc % 2].inc(g.dma_start(
                            out=w0_sb[hc % 2][:, :].rearrange("p (c h) -> p c h", c=16),
                            in_=w0_d[hc].rearrange("(c p) h -> p c h", p=128)), 16)

                for hc in range(NHC):
                    hp = hc % 2

                    @blk.tensor
                    def _(t, hc=hc, hp=hp):
                        t.wait_ge(g5[hc % 2].h, msf["w0"][hc])
                        if hc >= 2:
                            t.wait_ge(ss.h, msf["ev"][hc - 2])
                        for i in range(16):
                            for n in range(n_tmm):
                                n0, n1_ = n * 512, min(ROWS, (n + 1) * 512)
                                inst = t.matmul(
                                    h_ps[hp][:, n0:n1_],
                                    w0_sb[hp][:, i * 128:(i + 1) * 128],
                                    xnf2[:, i * ROWS + n0: i * ROWS + n1_],
                                    start=(i == 0), stop=(i == 15))
                        msf["mm"][hc] = st.inc(inst)

                    @blk.scalar
                    def _(s, hc=hc, hp=hp):
                        s.wait_ge(st.h, msf["mm"][hc])
                        msf["ev"][hc] = ss.inc(s.activation(
                            h_sb[:, hc * ROWS:(hc + 1) * ROWS], h_ps[hp][:, :],
                            AF.Gelu, bias=b0_sb[:, hc:hc + 1]))

                    @blk.gpsimd
                    def _(g, hc=hc):
                        if hc + 2 <= NHC - 1:
                            g.wait_ge(st.h, msf["mm"][hc])
                            msf["w0"][hc + 2] = g5[hc % 2].inc(g.dma_start(
                                out=w0_sb[hc % 2][:, :].rearrange(
                                    "p (c h) -> p c h", c=16),
                                in_=w0_d[hc + 2].rearrange("(c p) h -> p c h", p=128)),
                                16)

        etok = xnf2

        with contextlib.ExitStack() as es6:
            w1_sb = [es6.enter_context(
                nc.sbuf_tensor(f"w1_sb{i}", [128, NHC * 128], BF16)) for i in range(2)]
            e_sb = [es6.enter_context(
                nc.sbuf_tensor(f"e_sb{i}", [128, ROWS], BF16)) for i in range(2)]
            e_ps = [es6.enter_context(
                nc.psum_tensor(f"e_ps{i}", [128, ROWS], F32)) for i in range(2)]
            et_ps = [es6.enter_context(
                nc.psum_tensor(f"et_ps{i}", [128, ROWS], BF16)) for i in range(2)]
            msg = {"w1": {}, "mm": {}, "ev": {}, "tr": {}, "et": {}}
            with nc.Block() as blk:
                @blk.gpsimd
                def _(g):
                    for oc in range(min(2, NOC)):
                        msg["w1"][oc] = g6[oc % 2].inc(g.dma_start(
                            out=w1_sb[oc % 2][:, :].rearrange(
                                "p (c o) -> p c o", c=NHC),
                            in_=w1_d[oc].rearrange("(c p) o -> p c o", p=128)), 16)

                for oc in range(NOC):
                    op = oc % 2

                    @blk.tensor
                    def _(t, oc=oc, op=op):
                        t.wait_ge(g6[oc % 2].h, msg["w1"][oc])
                        if oc >= 2:
                            t.wait_ge(ss.h, msg["ev"][oc - 2])
                        for hc in range(NHC):
                            for n in range(n_tmm):
                                n0, n1_ = n * 512, min(ROWS, (n + 1) * 512)
                                inst = t.matmul(
                                    e_ps[op][:, n0:n1_],
                                    w1_sb[op][:, hc * 128:(hc + 1) * 128],
                                    h_sb[:, hc * ROWS + n0: hc * ROWS + n1_],
                                    start=(hc == 0), stop=(hc == NHC - 1))
                        msg["mm"][oc] = st.inc(inst)

                    @blk.scalar
                    def _(s, oc=oc, op=op):
                        s.wait_ge(st.h, msg["mm"][oc])
                        if oc >= 2:
                            s.wait_ge(st.h, msg["tr"][oc - 2])
                        msg["ev"][oc] = ss.inc(s.activation(
                            e_sb[op][:, :], e_ps[op][:, :], AF.Identity,
                            bias=b1_sb[:, oc:oc + 1]))

                    @blk.tensor
                    def _(t, oc=oc, op=op):
                        t.wait_ge(ss.h, msg["ev"][oc])
                        if oc >= 2:
                            t.wait_ge(sv.h, msg["et"][oc - 2])
                        for tb in range(NTC):
                            inst = t.transpose(
                                et_ps[op][:, tb * 128:(tb + 1) * 128],
                                e_sb[op][:, tb * 128:(tb + 1) * 128],
                                id_sb[:, :])
                        msg["tr"][oc] = st.inc(inst)

                    @blk.vector
                    def _(v, oc=oc, op=op):
                        v.wait_ge(st.h, msg["tr"][oc])
                        dst = pap(etok, oc * 128, [[D, NTC], [1, 128]])
                        src = pap(et_ps[op], 0, [[128, NTC], [1, 128]])
                        msg["et"][oc] = sv.inc(v.tensor_copy(dst, src))

                    @blk.gpsimd
                    def _(g, oc=oc):
                        if oc + 2 <= NOC - 1:
                            g.wait_ge(st.h, msg["mm"][oc])
                            msg["w1"][oc + 2] = g6[oc % 2].inc(g.dma_start(
                                out=w1_sb[oc % 2][:, :].rearrange(
                                    "p (c o) -> p c o", c=NHC),
                                in_=w1_d[oc + 2].rearrange("(c p) o -> p c o", p=128)),
                                16)

        with contextlib.ExitStack() as es7:
            def sb7(name, shape, dt=F32):
                return es7.enter_context(nc.sbuf_tensor(name, shape, dt))

            o1l = [sb7(f"o1l{i}", [128, D], BF16) for i in range(2)]
            junk7 = [sb7(f"junk7_{i}", [128, D], BF16) for i in range(2)]
            tmp7 = [sb7(f"tmp7_{i}", [128, D], BF16) for i in range(2)]
            outb = [sb7(f"outb{i}", [128, D], F32) for i in range(2)]
            st7 = [[sb7(f"st7_{i}_{k}", [128, 1], F32) for k in range(7)]
                   for i in range(2)]

            mse = {k: {} for k in ("o1l", "stat", "var", "sd", "tmp", "out", "dma")}

            with nc.Block() as blk:
                @blk.sync
                def _(q):
                    for u in range(min(2, NTC)):
                        mse["o1l"][u] = q7[u % 2].inc(q.dma_start(
                            out=o1l[u % 2][:, :],
                            in_=out1_d[u * 128:(u + 1) * 128, :]), 16)

                for u in range(NTC):
                    up = u % 2

                    @blk.scalar
                    def _(s, u=u, up=up):
                        if u >= 2:
                            s.wait_ge(sv.h, mse["var"][u - 2])
                        ech = etok[:, u * D:(u + 1) * D]
                        ss.inc(s.activation(junk7[up][:, :], ech, AF.Square,
                                            accum_out=st7[up][1][:, :]))
                        ss.wait(s)
                        mse["stat"][u] = ss.inc(s.activation(
                            junk7[up][:, :], ech, AF.Identity,
                            accum_out=st7[up][0][:, :]))

                    @blk.vector
                    def _(v, u=u, up=up):
                        v.wait_ge(ss.h, mse["stat"][u])
                        sv.inc(v.tensor_scalar(st7[up][2][:, :], st7[up][0][:, :],
                                               -1.0 / D, None, ALU.mult))
                        sv.wait(v)
                        sv.inc(v.tensor_tensor(st7[up][3][:, :], st7[up][2][:, :],
                                               st7[up][2][:, :], ALU.mult))
                        sv.wait(v)
                        mse["var"][u] = sv.inc(v.scalar_tensor_tensor(
                            st7[up][4][:, :], st7[up][1][:, :], 1.0 / D,
                            st7[up][3][:, :], ALU.mult, ALU.subtract))

                    @blk.scalar
                    def _(s, u=u, up=up):
                        s.wait_ge(sv.h, mse["var"][u])
                        mse["sd"][u] = ss.inc(s.activation(
                            st7[up][5][:, :], st7[up][4][:, :], AF.Sqrt, bias=EPS_LN))

                    @blk.vector
                    def _(v, u=u, up=up):
                        v.wait_ge(ss.h, mse["sd"][u])
                        sv.inc(v.reciprocal(st7[up][6][:, :], st7[up][5][:, :]))
                        sv.wait(v)
                        if u >= 2:
                            v.wait_ge(sv.h, mse["out"][u - 2])
                        mse["tmp"][u] = sv.inc(v.tensor_scalar(
                            tmp7[up][:, :], etok[:, u * D:(u + 1) * D],
                            st7[up][2][:, :], st7[up][6][:, :], ALU.add, ALU.mult))
                        v.wait_ge(sv.h, mse["tmp"][u])
                        v.wait_ge(q7[u % 2].h, mse["o1l"][u])
                        if u >= 2:
                            v.wait_ge(q8[u % 2].h, mse["dma"][u - 2])
                        mse["out"][u] = sv.inc(v.tensor_tensor(
                            outb[up][:, :], tmp7[up][:, :], o1l[up][:, :], ALU.add))

                    @blk.sync
                    def _(q, u=u, up=up):
                        q.wait_ge(sv.h, mse["out"][u])
                        mse["dma"][u] = q8[u % 2].inc(q.dma_start(
                            out=out_d[u * 128:(u + 1) * 128, :],
                            in_=outb[up][:, :]), 16)
                        if u + 2 <= NTC - 1:
                            mse["o1l"][u + 2] = q7[u % 2].inc(q.dma_start(
                                out=o1l[up][:, :],
                                in_=out1_d[(u + 2) * 128:(u + 3) * 128, :]), 16)

                @blk.sync
                def _(q):
                    q8[0].wait(q)
                    q8[1].wait(q)

    return nc


def _host_inputs(L0_attn_x, L0_attn_Wq, L0_attn_Wk, L0_attn_Wv, L0_attn_Wo,
                 L0_ffn_W0, L0_ffn_bias0, L0_ffn_W1, L0_ffn_bias1):
    x2d = np.ascontiguousarray(np.asarray(L0_attn_x, dtype=np.float32)[0])
    SEQ = x2d.shape[0]
    ROWS = SEQ // NCORE
    NTT = SEQ // 128

    WqT = np.ascontiguousarray(np.asarray(L0_attn_Wq, np.float32).T)
    WkT = np.ascontiguousarray(np.asarray(L0_attn_Wk, np.float32).T)
    WvT = np.ascontiguousarray(np.asarray(L0_attn_Wv, np.float32).T)
    WoT = np.ascontiguousarray(np.asarray(L0_attn_Wo, np.float32).T)
    w0h = np.ascontiguousarray(
        np.asarray(L0_ffn_W0, np.float32).T.reshape(D, DFF // 128, 128)
        .transpose(1, 0, 2))
    w1o = np.ascontiguousarray(
        np.asarray(L0_ffn_W1, np.float32).T.reshape(DFF, D // 128, 128)
        .transpose(1, 0, 2))
    b0c = np.ascontiguousarray(
        np.asarray(L0_ffn_bias0, np.float32).reshape(DFF // 128, 128).T)
    b1c = np.ascontiguousarray(
        np.asarray(L0_ffn_bias1, np.float32).reshape(D // 128, 128).T)

    half = HD // 2
    theta = 10000.0 ** (-np.arange(half, dtype=np.float64) * 2.0 / HD)
    ang = np.arange(SEQ, dtype=np.float64)[:, None] * theta[None, :]
    cosT = np.cos(ang).astype(np.float32).reshape(NTT, 128, half)
    sinT = np.sin(ang).astype(np.float32).reshape(NTT, 128, half)

    identb = np.eye(128, dtype=BFNP)
    maskb = np.triu(np.ones((128, 128), dtype=np.float32))
    onesb = np.ones((128, 128), dtype=BFNP)

    in_maps = []
    for c in range(NCORE):
        in_maps.append({
            "x": x2d,
            "xr": np.ascontiguousarray(x2d[c * ROWS:(c + 1) * ROWS]),
            "wq": np.ascontiguousarray(WqT[:, 256 * c:256 * (c + 1)]),
            "wk": np.ascontiguousarray(WkT[:, 256 * c:256 * (c + 1)]),
            "wv": np.ascontiguousarray(WvT[:, 256 * c:256 * (c + 1)]),
            "wo": WoT,
            "w0": w0h,
            "w1": w1o,
            "b0": b0c,
            "b1": b1c,
            "cosT": cosT,
            "sinT": sinT,
            "identb": identb,
            "maskb": maskb,
            "onesb": onesb,
        })
    return SEQ, in_maps


def kernel(L0_attn_x, L0_attn_Wq, L0_attn_Wk, L0_attn_Wv, L0_attn_Wo,
           L0_ffn_W0, L0_ffn_bias0, L0_ffn_W1, L0_ffn_bias1):
    SEQ, in_maps = _host_inputs(
        L0_attn_x, L0_attn_Wq, L0_attn_Wk, L0_attn_Wv, L0_attn_Wo,
        L0_ffn_W0, L0_ffn_bias0, L0_ffn_W1, L0_ffn_bias1)
    nc = build_nc(SEQ)
    res = run_bass_kernel_spmd(nc, in_maps, core_ids=list(range(NCORE)))
    out = np.concatenate([np.asarray(res.results[c]["out"])
                          for c in range(NCORE)], axis=0)
    return out.reshape(1, SEQ, D).astype(np.float32)
